# version 02 (final)
# baseline (speedup 1.0000x reference)
import numpy as np
import ml_dtypes
from contextlib import ExitStack
import concourse.bass as bass
import concourse.mybir as mybir

F32 = mybir.dt.float32
BF16 = mybir.dt.bfloat16
AF = mybir.ActivationFunctionType
ALU = mybir.AluOpType
AX = mybir.AxisListType

ENGS = ("sync", "gpsimd", "tensor", "vector", "scalar")
D, H, HD, NK, P, NQT, NE, NI, NM = 2048, 16, 128, 16, 1024, 8, 32, 768, 6
SCALE = float(1.0 / np.sqrt(128.0))


class Prog:
    def __init__(self, nc, sems):
        self.nc = nc
        self.ops = {e: [] for e in ENGS}
        self.sems = sems
        self.cnt = {k: 0 for k in sems}

    def wait(self, eng, mark):
        if mark is None:
            return
        if isinstance(mark, (list, tuple)) and mark and isinstance(mark[0], tuple):
            for m in mark:
                self.wait(eng, m)
            return
        sem, val = mark
        if val <= 0:
            return
        h = self.sems[sem]
        self.ops[eng].append(lambda e: e.wait_ge(h, val))

    def emit(self, eng, fn, inc=None, amt=1):
        if inc is None:
            self.ops[eng].append(fn)
            return None
        h = self.sems[inc]
        self.cnt[inc] += amt
        v = self.cnt[inc]
        self.ops[eng].append(lambda e: fn(e).then_inc(h, amt))
        return (inc, v)

    def mark(self, sem):
        return (sem, self.cnt[sem])

    def wait_dma_all(self, eng):
        for s in self.cnt:
            if s.startswith("d_"):
                self.wait(eng, self.mark(s))

    def barrier(self):
        engines = ("tensor", "vector", "scalar")
        sem_of = {"tensor": "s_pe", "vector": "s_dve", "scalar": "s_act"}
        marks = {eng: self.mark(sem_of[eng]) for eng in engines}
        for eng in engines:
            for other in engines:
                if other != eng:
                    self.wait(eng, marks[other])


class Buf:
    __slots__ = ("ap", "ready", "free")

    def __init__(self, ap):
        self.ap = ap
        self.ready = None
        self.free = None


def host_prepare(inputs):
    f = np.ascontiguousarray
    d = {}
    x = np.asarray(inputs["L0_attn_a_x"], dtype=np.float32)
    d["_x_seqs"] = [f(x[b]) for b in range(4)]

    Wq = np.asarray(inputs["L0_attn_a_Wq"], np.float32)
    Wk = np.asarray(inputs["L0_attn_a_Wk"], np.float32)
    Wv = np.asarray(inputs["L0_attn_a_Wv"], np.float32)
    Wo = np.asarray(inputs["L0_attn_a_Wo"], np.float32)
    per_head = lambda W: f(W.reshape(H, NK, 128, HD).transpose(0, 2, 1, 3))
    d["a_wq"], d["a_wk"], d["a_wv"] = per_head(Wq), per_head(Wk), per_head(Wv)
    d["a_wo"] = f(Wo.reshape(NK, 128, D))
    d["a_bq"] = f(np.asarray(inputs["L0_attn_a_bq"], np.float32).reshape(1, D))
    d["a_bk"] = f(np.asarray(inputs["L0_attn_a_bk"], np.float32).reshape(1, D))
    d["a_bv"] = f(np.asarray(inputs["L0_attn_a_bv"], np.float32).reshape(NK, 128).T)
    d["a_bo"] = f(np.asarray(inputs["L0_attn_a_bo"], np.float32).reshape(1, D))

    wqkv = np.asarray(inputs["L0_attn_b_w_qkv"], np.float32)
    ph_t = lambda W: f(W.T.reshape(NK, 128, H, HD).transpose(2, 1, 0, 3))
    d["b_wq"], d["b_wk"], d["b_wv"] = ph_t(wqkv[0:D]), ph_t(wqkv[D:2 * D]), ph_t(wqkv[2 * D:3 * D])
    wout = np.asarray(inputs["L0_attn_b_w_out"], np.float32)
    d["b_wo"] = f(wout.T.reshape(NK, 128, D))
    bqkv = np.asarray(inputs["L0_attn_b_b_qkv"], np.float32)
    d["b_bq"] = f(bqkv[0:D].reshape(1, D))
    d["b_bk"] = f(bqkv[D:2 * D].reshape(1, D))
    d["b_bv"] = f(bqkv[2 * D:3 * D].reshape(NK, 128).T)
    d["b_bo"] = f(np.asarray(inputs["L0_attn_b_b_out"], np.float32).reshape(1, D))

    wg = np.asarray(inputs["L0_ffn_w_gate"], np.float32)
    wu = np.asarray(inputs["L0_ffn_w_up"], np.float32)
    wd = np.asarray(inputs["L0_ffn_w_down"], np.float32)
    d["f_wg"] = f(wg.reshape(NE, NK, 128, NM, 128).transpose(0, 3, 2, 1, 4))
    d["f_wu"] = f(wu.reshape(NE, NK, 128, NM, 128).transpose(0, 3, 2, 1, 4))
    d["f_wd"] = f(wd.reshape(NE, NM, 128, 2, 1024).transpose(0, 3, 2, 1, 4))
    rw = np.asarray(inputs["L0_ffn_router_w"], np.float32)
    d["f_rw"] = f(rw.T.reshape(NK, 128, NE))

    d["c_id_f32"] = np.eye(128, dtype=np.float32)
    d["c_diag"] = (np.arange(128)[None, :] >= np.arange(128)[:, None]).astype(np.float32)
    return d


IN_SPECS = [
    ("xin", (2048, D), F32),
    ("f_wg", (NE, NM, 128, NK, 128), F32), ("f_wu", (NE, NM, 128, NK, 128), F32),
    ("f_wd", (NE, 2, 128, NM, 1024), F32), ("f_rw", (NK, 128, NE), F32),
    ("c_id_f32", (128, 128), F32), ("c_diag", (128, 128), F32),
]
for _l in ("a", "b"):
    IN_SPECS += [(_l + "_wq", (H, 128, NK, HD), F32), (_l + "_wk", (H, 128, NK, HD), F32),
                 (_l + "_wv", (H, 128, NK, HD), F32), (_l + "_wo", (NK, 128, D), F32),
                 (_l + "_bq", (1, D), F32), (_l + "_bk", (1, D), F32),
                 (_l + "_bv", (128, NK), F32), (_l + "_bo", (1, D), F32)]


class K:
    def __init__(self, test_subs=("attn_a", "ffn", "attn_b", "ffn2"), passes=(0, 1)):
        self.test_subs = test_subs
        self.passes = passes

    def build(self):
        nc = bass.Bass("TRN2", target_bir_lowering=False)
        self.nc = nc
        self.ap = {}
        for name, shape, dt in IN_SPECS:
            self.ap[name] = nc.dram_tensor(name, list(shape), dt, kind="ExternalInput").ap()
        self.out = nc.dram_tensor("o", [2048, D], F32, kind="ExternalOutput").ap()
        self.kstore = {l: nc.dram_tensor(f"kst_{l}", [H, 128, P], BF16, kind="Internal").ap()
                       for l in ("a", "b")}
        self.vstore = {l: nc.dram_tensor(f"vst_{l}", [NQT, 128, D], BF16, kind="Internal").ap()
                       for l in ("a", "b")}
        self.ystore = nc.dram_tensor("yst", [H, 128, P], BF16, kind="Internal").ap()

        es = ExitStack()
        self.es = es
        sems = {}
        semnames = ["s_pe", "s_dve", "s_act", "s_out",
                    "d_const", "d_xin", "d_bias", "d_rw",
                    "d_wq0", "d_wq1", "d_wk0", "d_wk1", "d_wv0", "d_wv1",
                    "d_wof", "d_wg0", "d_wg1", "d_wu0", "d_wu1", "d_wd0", "d_wd1",
                    "d_kst", "d_kld", "d_yst", "d_yqt0", "d_yqt1"]
        semnames += [f"d_vo{i}" for i in range(NQT)]
        semnames += [f"d_vs{i}" for i in range(NQT)]
        for s in semnames:
            sems[s] = es.enter_context(nc.semaphore(s))
        with es:
            self.p = Prog(nc, sems)
            self.alloc_persistent()
            self.emit_all()
            with nc.Block() as block:
                for eng in ENGS:
                    def run(e, eng=eng):
                        for fn in self.p.ops[eng]:
                            fn(e)
                    getattr(block, eng)(run)
        return nc

    def sb(self, n, s, dt):
        self._uid = getattr(self, "_uid", 0) + 1
        return self.es.enter_context(self.nc.sbuf_tensor(f"{n}_u{self._uid}", list(s), dt)).ap()

    def psum(self, n, s, dt=F32):
        return self.es.enter_context(self.nc.psum_tensor(n, list(s), dt)).ap()

    def alloc_persistent(self):
        sb = self.sb
        B = Buf
        self.x_res = sb("x_res", (128, NQT * D), F32)
        self.xlnT = sb("xlnT", (128, NK * P), BF16)
        self.rw_sb = sb("rw_sb", (128, NK * NE), BF16)
        self.id_bf = sb("id_bf", (128, 128), BF16)
        self.id_f32 = sb("id_f32", (128, 128), F32)
        self.diag = sb("diag", (128, 128), BF16)
        self.ones_row = sb("ones_row", (1, 512), BF16)
        self.ones_col = sb("ones_col", (128, 1), BF16)
        self.onesf_row = sb("onesf_row", (1, 128), F32)
        self.c_eps = sb("c_eps", (128, 1), F32)
        self.c_nh = sb("c_nh", (128, 1), F32)
        self.c_neg1 = sb("c_neg1", (128, 1), F32)
        self.c_scale = sb("c_scale", (128, 1), F32)
        self.lnscr = [B(sb(f"lnscr{i}", (128, D), F32)) for i in range(2)]
        self.xln_tm = [B(sb(f"xlntm{i}", (128, D), BF16)) for i in range(2)]
        self.stat = [sb(f"stat{i}", (128, 8), F32) for i in range(2)]
        self.psA = B(self.psum("psA", (128, P)))
        self.psB = B(self.psum("psB", (128, P)))
        self.psD = B(self.psum("psD", (128, P)))
        self.putil = B(self.psum("putil", (128, 1024), BF16))
        pall = self.psum("pall", (128, 512), F32)
        self.pxbuf = B(pall)
        self.psR = B(pall)
        self.psT = B(pall[:, 0:NE]); self.psT.free = None
        self.psNB = B(pall[:, 32:33])
        self.psS = B(pall[0:1, 0:128])
        self.psU = self.putil
        self.psV = self.putil

    def emit_all(self):
        p = self.p
        for name, dst in (("c_id_f32", self.id_bf), ("c_id_f32", self.id_f32),
                          ("c_diag", self.diag)):
            p.emit("gpsimd", lambda e, n=name, d=dst: e.dma_start(out=d[:], in_=self.ap[n][:]),
                   inc="d_const", amt=16)
        p.emit("gpsimd", lambda e: e.dma_start(
            out=self.rw_sb[:].rearrange("p (k j) -> p k j", k=NK),
            in_=self.ap["f_rw"][:].rearrange("k p j -> p k j")),
            inc="d_const", amt=16)
        cm = p.mark("d_const")
        for eng in ("vector", "scalar", "tensor"):
            p.wait(eng, cm)
        for t, v in ((self.ones_row, 1.0), (self.ones_col, 1.0), (self.onesf_row, 1.0),
                     (self.c_eps, 1e-5), (self.c_nh, -0.5), (self.c_neg1, -1.0),
                     (self.c_scale, SCALE)):
            p.emit("vector", lambda e, t=t, v=v: e.memset(t[:], v), inc="s_dve")
        cd = p.mark("s_dve")
        for eng in ("scalar", "tensor"):
            p.wait(eng, cd)

        for pas in self.passes:
            tb = pas * P
            p.wait("gpsimd", p.mark("s_out"))
            xm = p.emit("gpsimd", lambda e, tb=tb: e.dma_start(
                out=self.x_res[:].rearrange("p (a d) -> p a d", a=NQT),
                in_=self.ap["xin"][tb:tb + P].rearrange("(a p) d -> p a d", p=128)),
                inc="d_xin", amt=16)
            for eng in ("vector", "scalar", "tensor"):
                p.wait(eng, xm)
            for sub in self.test_subs:
                layer = {"attn_a": "a", "attn_b": "b"}.get(sub)
                with ExitStack() as scope:
                    old_es, self.es = self.es, scope
                    self.emit_ln()
                    if layer is not None:
                        self.emit_attn(layer, pas)
                    else:
                        self.emit_ffn(pas)
                    self.es = old_es
                p.barrier()
                for q in ("gpsimd", "sync"):
                    p.wait(q, p.mark("s_pe"))
                    p.wait(q, p.mark("s_dve"))
                    p.wait(q, p.mark("s_act"))
                    p.wait_dma_all(q)
            p.wait("gpsimd", p.mark("s_dve"))
            p.wait("gpsimd", p.mark("s_act"))
            p.wait("gpsimd", p.mark("d_kst"))
            p.emit("gpsimd", lambda e, tb=tb: e.dma_start(
                out=self.out[tb:tb + P].rearrange("(a p) d -> p a d", p=128),
                in_=self.x_res[:].rearrange("p (a d) -> p a d", a=NQT)),
                inc="s_out", amt=16)
        p.wait("gpsimd", p.mark("s_out"))

    def emit_ln(self):
        p = self.p
        p.wait("vector", p.mark("s_dve"))
        p.wait("scalar", p.mark("s_dve"))
        for qt in range(NQT):
            sl = qt % 2
            st = self.stat[sl]
            scr, xtm = self.lnscr[sl], self.xln_tm[sl]
            xa = self.x_res[:, qt * D:(qt + 1) * D]
            m_sum = p.emit("vector", lambda e, xa=xa, st=st: e.reduce_sum(
                out=st[:, 0:1], in_=xa, axis=AX.X), inc="s_dve")
            p.wait("scalar", scr.free)
            a_sq = p.emit("scalar", lambda e, xa=xa, scr=scr: e.activation(
                out=scr.ap[:], in_=xa, func=AF.Square), inc="s_act")
            p.wait("vector", a_sq)
            m_sq = p.emit("vector", lambda e, st=st, scr=scr: e.reduce_sum(
                out=st[:, 1:2], in_=scr.ap[:], axis=AX.X), inc="s_dve")
            scr.free = m_sq
            p.wait("vector", m_sum)
            m_m = p.emit("vector", lambda e, st=st: e.tensor_scalar(
                out=st[:, 2:3], in0=st[:, 0:1], scalar1=1.0 / D, scalar2=None,
                op0=ALU.mult), inc="s_dve")
            p.wait("vector", m_m)
            m_m2 = p.emit("vector", lambda e, st=st: e.scalar_tensor_tensor(
                out=st[:, 3:4], in0=st[:, 2:3], scalar=-1.0, in1=st[:, 2:3],
                op0=ALU.mult, op1=ALU.mult), inc="s_dve")
            p.wait("vector", m_m2)
            m_var = p.emit("vector", lambda e, st=st: e.scalar_tensor_tensor(
                out=st[:, 4:5], in0=st[:, 1:2], scalar=1.0 / D, in1=st[:, 3:4],
                op0=ALU.mult, op1=ALU.add), inc="s_dve")
            p.wait("scalar", m_var)
            a_ln = p.emit("scalar", lambda e, st=st: e.activation(
                out=st[:, 5:6], in_=st[:, 4:5], func=AF.Ln, bias=self.c_eps[:]), inc="s_act")
            p.wait("scalar", a_ln)
            a_rs = p.emit("scalar", lambda e, st=st: e.activation(
                out=st[:, 6:7], in_=st[:, 5:6], func=AF.Exp, scale=self.c_nh[:]), inc="s_act")
            p.wait("vector", a_rs)
            m_b2 = p.emit("vector", lambda e, st=st: e.scalar_tensor_tensor(
                out=st[:, 7:8], in0=st[:, 2:3], scalar=-1.0, in1=st[:, 6:7],
                op0=ALU.mult, op1=ALU.mult), inc="s_dve")
            p.wait("scalar", m_b2)
            p.wait("scalar", a_rs)
            p.wait("scalar", xtm.free)
            a_x = p.emit("scalar", lambda e, xa=xa, st=st, xtm=xtm: e.activation(
                out=xtm.ap[:], in_=xa, func=AF.Identity, scale=st[:, 6:7],
                bias=st[:, 7:8]), inc="s_act")
            p.wait("tensor", a_x)
            last_cp = None
            for grp in range(2):
                pu = self.putil
                p.wait("tensor", pu.free)
                t_k = None
                for j in range(8):
                    k = grp * 8 + j
                    t_k = p.emit("tensor", lambda e, k=k, j=j, xtm=xtm, pu=pu: e.transpose(
                        out=pu.ap[:, j * 128:(j + 1) * 128],
                        in_=xtm.ap[:, k * 128:(k + 1) * 128],
                        identity=self.id_bf[:]), inc="s_pe" if j == 7 else None)
                p.wait("vector", t_k)
                dst = self.xlnT[:].rearrange("p (k c) -> p k c", k=NK)[
                    :, grp * 8:(grp + 1) * 8, qt * 128: qt * 128 + 128]
                c_k = p.emit("vector", lambda e, pu=pu, dst=dst: e.tensor_copy(
                    out=dst, in_=pu.ap[:].rearrange("p (j c) -> p j c", j=8)),
                    inc="s_dve")
                pu.free = c_k
                last_cp = c_k
            xtm.free = last_cp
        self.xlnT_ready = p.mark("s_dve")

    def emit_attn(self, layer, pas):
        p = self.p
        sb = self.sb
        B = Buf
        NST = NQT * (pas + 1)
        kst, vst, yst = self.kstore[layer], self.vstore[layer], self.ystore
        xlnT = self.xlnT
        psA, psB, psD, psR = self.psA, self.psB, self.psD, self.psR
        psU, psV, psS, psNB = self.psU, self.psV, self.psS, self.psNB
        ymarks = {}

        with ExitStack() as heads_scope:
            outer_es, self.es = self.es, heads_scope
            whq = [B(sb(f"whq{i}", (128, NK * HD), BF16)) for i in range(2)]
            whk = [B(sb(f"whk{i}", (128, NK * HD), BF16)) for i in range(2)]
            whv = [B(sb(f"whv{i}", (128, NK * HD), BF16)) for i in range(2)]
            bq_sb = sb("bq_sb", (1, D), BF16)
            bk_sb = sb("bk_sb", (1, D), BF16)
            qT = [B(sb(f"qT{i}", (128, P), BF16)) for i in range(2)]
            kT = B(sb("kT", (128, 2 * P), BF16))
            vTs = B(sb("vTs", (128, P), BF16))
            vown = [B(sb(f"vown{i}", (128, HD), BF16)) for i in range(NQT)]
            vstr = [B(sb(f"vstr{i}", (128, HD), BF16)) for i in range(NQT)]
            scf = [B(sb(f"scf{i}", (128, P), F32)) for i in range(2)]
            pt = [B(sb(f"pt{i}", (128, P), BF16)) for i in range(2)]
            mu = sb("mu", (128, 16), F32)
            m1col = B(sb("m1col", (128, 1), F32))
            murow = B(sb("murow", (1, 128), F32))
            nsmall = B(sb("nsmall", (1, 1), F32))
            nb = B(sb("nb", (128, 1), F32))
            rrow = B(sb("rrow", (1, P), F32))
            rec = B(sb("rec", (1, P), F32))
            recb = B(sb("recb", (128, P), F32))
            yev = B(sb("yev", (128, P), BF16))

            for n, dst in ((layer + "_bq", bq_sb), (layer + "_bk", bk_sb)):
                p.emit("gpsimd", lambda e, n=n, d=dst: e.dma_start(out=d[:], in_=self.ap[n][:]),
                       inc="d_bias", amt=16)
            bias_m = p.mark("d_bias")
            p.wait("tensor", bias_m)
            p.wait("tensor", self.xlnT_ready)

            for h in range(H):
                sq, sk, sv = whq[h % 2], whk[h % 2], whv[h % 2]
                for nm, s, sem in ((layer + "_wq", sq, f"d_wq{h % 2}"),
                                   (layer + "_wk", sk, f"d_wk{h % 2}"),
                                   (layer + "_wv", sv, f"d_wv{h % 2}")):
                    p.wait("gpsimd", s.free)
                    s.ready = p.emit("gpsimd", lambda e, n=nm, h=h, s=s: e.dma_start(
                        out=s.ap[:].rearrange("p (k d) -> p k d", k=NK), in_=self.ap[n][h]),
                        inc=sem, amt=16)
                qb = qT[h % 2]

                p.wait("tensor", sq.ready)
                p.wait("tensor", psA.free)
                p.wait("tensor", qb.free)
                for k in range(NK):
                    for c in range(2):
                        p.emit("tensor", lambda e, k=k, c=c, sq=sq: e.matmul(
                            out=psA.ap[:, c * 512:(c + 1) * 512],
                            lhsT=sq.ap[:, k * 128:(k + 1) * 128],
                            rhs=xlnT[:, k * P + c * 512: k * P + (c + 1) * 512],
                            start=(k == 0), stop=False))
                bm = None
                for c in range(2):
                    bm = p.emit("tensor", lambda e, c=c, h=h: e.matmul(
                        out=psA.ap[:, c * 512:(c + 1) * 512],
                        lhsT=bq_sb[0:1, h * 128:(h + 1) * 128],
                        rhs=self.ones_row[0:1, :], start=False, stop=True),
                        inc="s_pe" if c == 1 else None)
                sq.free = bm
                p.wait("scalar", bm)
                qev = p.emit("scalar", lambda e, qb=qb: e.activation(
                    out=qb.ap[:], in_=psA.ap[:], func=AF.Identity, scale=self.c_scale[:]),
                    inc="s_act")
                qb.ready = qev
                psA.free = qev

                p.wait("tensor", sk.ready)
                p.wait("tensor", psB.free)
                p.wait("tensor", kT.free)
                for k in range(NK):
                    for c in range(2):
                        p.emit("tensor", lambda e, k=k, c=c, sk=sk: e.matmul(
                            out=psB.ap[:, c * 512:(c + 1) * 512],
                            lhsT=sk.ap[:, k * 128:(k + 1) * 128],
                            rhs=xlnT[:, k * P + c * 512: k * P + (c + 1) * 512],
                            start=(k == 0), stop=False))
                for c in range(2):
                    bm = p.emit("tensor", lambda e, c=c, h=h: e.matmul(
                        out=psB.ap[:, c * 512:(c + 1) * 512],
                        lhsT=bk_sb[0:1, h * 128:(h + 1) * 128],
                        rhs=self.ones_row[0:1, :], start=False, stop=True),
                        inc="s_pe" if c == 1 else None)
                sk.free = bm
                p.wait("scalar", bm)
                kev = p.emit("scalar", lambda e, pas=pas, kT=kT: e.activation(
                    out=kT.ap[:, pas * P:(pas + 1) * P], in_=psB.ap[:], func=AF.Identity),
                    inc="s_act")
                psB.free = kev
                kready = [kev]
                if pas == 0:
                    p.wait("gpsimd", kev)
                    kstm = p.emit("gpsimd", lambda e, h=h, kT=kT: e.dma_start(
                        out=kst[h], in_=kT.ap[:, 0:P]), inc="d_kst", amt=16)
                else:
                    if h == 0:
                        p.wait("sync", p.mark("d_kst"))
                    p.wait("sync", kT.free)
                    kld = p.emit("sync", lambda e, h=h, kT=kT: e.dma_start(
                        out=kT.ap[:, 0:P], in_=kst[h]), inc="d_kld", amt=16)
                    kready.append(kld)
                    kstm = None

                p.wait("tensor", sv.ready)
                p.wait("tensor", psD.free)
                mmv = None
                for k in range(NK):
                    for c in range(2):
                        mmv = p.emit("tensor", lambda e, k=k, c=c, sv=sv: e.matmul(
                            out=psD.ap[:, c * 512:(c + 1) * 512],
                            lhsT=sv.ap[:, k * 128:(k + 1) * 128],
                            rhs=xlnT[:, k * P + c * 512: k * P + (c + 1) * 512],
                            start=(k == 0), stop=(k == NK - 1)),
                            inc="s_pe" if (k == NK - 1 and c == 1) else None)
                sv.free = mmv
                p.wait("scalar", mmv)
                p.wait("scalar", vTs.free)
                vev = p.emit("scalar", lambda e, vTs=vTs: e.activation(
                    out=vTs.ap[:], in_=psD.ap[:], func=AF.Identity), inc="s_act")
                psD.free = vev
                p.wait("tensor", vev)
                pu = self.putil
                p.wait("tensor", pu.free)
                last_tv = None
                for st8 in range(NQT):
                    last_tv = p.emit("tensor", lambda e, st8=st8, pu=pu, vTs=vTs: e.transpose(
                        out=pu.ap[:, st8 * 128:(st8 + 1) * 128],
                        in_=vTs.ap[:, st8 * 128:(st8 + 1) * 128],
                        identity=self.id_bf[:]), inc="s_pe" if st8 == NQT - 1 else None)
                p.wait("vector", last_tv)
                for st8 in range(NQT):
                    p.wait("vector", vown[st8].free)
                c_v = None
                for st8 in range(NQT):
                    vt = vown[st8]
                    c_v = p.emit("vector", lambda e, vt=vt, pu=pu, st8=st8: e.tensor_copy(
                        out=vt.ap[:], in_=pu.ap[:, st8 * 128:(st8 + 1) * 128]),
                        inc="s_dve" if st8 == NQT - 1 else None)
                    vt.ready = c_v
                for st8 in range(NQT):
                    vown[st8].ready = c_v
                pu.free = c_v
                if pas == 0:
                    p.wait("gpsimd", c_v)
                    for st8 in range(NQT):
                        vt = vown[st8]
                        dm = p.emit("gpsimd", lambda e, st8=st8, h=h, vt=vt: e.dma_start(
                            out=vst[st8][:, h * 128:(h + 1) * 128], in_=vt.ap[:]),
                            inc=f"d_vo{st8}", amt=16)
                        vt.free = dm
                vTs.free = last_tv
                if pas == 1:
                    if h == 0:
                        for i in range(NQT):
                            p.wait("sync", p.mark(f"d_vo{i}"))
                    for st8 in range(NQT):
                        vt = vstr[st8]
                        p.wait("sync", vt.free)
                        vt.ready = p.emit("sync", lambda e, st8=st8, h=h, vt=vt: e.dma_start(
                            out=vt.ap[:], in_=vst[st8][:, h * 128:(h + 1) * 128]),
                            inc=f"d_vs{st8}", amt=16)

                p.wait("tensor", qev)
                for m in kready:
                    p.wait("tensor", m)
                last_mu = None
                for st in range(NST):
                    qs = max(0, (st - 8 * pas) * 128)
                    pp = psA if st % 2 == 0 else psB
                    p.wait("tensor", pp.free)
                    cols = [(c0, min(c0 + 512, P)) for c0 in range((qs // 512) * 512, P, 512)]
                    mms = None
                    for (c0, c1) in cols:
                        mms = p.emit("tensor", lambda e, st=st, c0=c0, c1=c1, pp=pp, qb=qb, kT=kT:
                                     e.matmul(out=pp.ap[:, c0:c1],
                                              lhsT=kT.ap[:, st * 128:(st + 1) * 128],
                                              rhs=qb.ap[:, c0:c1], start=True, stop=True),
                                     inc="s_pe" if (c0, c1) == cols[-1] else None)
                    sc = scf[st % 2]
                    p.wait("vector", mms)
                    p.wait("vector", sc.free)
                    cpy = p.emit("vector", lambda e, sc=sc, pp=pp, qs=qs: e.tensor_copy(
                        out=sc.ap[:, qs:P], in_=pp.ap[:, qs:P]), inc="s_dve")
                    pp.free = cpy
                    p.wait("vector", cpy)
                    last_mu = p.emit("vector", lambda e, st=st, sc=sc, qs=qs: e.reduce_max(
                        out=mu[:, st % 16:(st % 16) + 1], in_=sc.ap[:, qs:P], axis=AX.X),
                        inc="s_dve")
                    sc.free = last_mu
                p.wait("vector", last_mu)
                p.wait("vector", m1col.free)
                r1 = p.emit("vector", lambda e, n=NST: e.reduce_max(
                    out=m1col.ap[:], in_=mu[:, 0:n], axis=AX.X), inc="s_dve")
                p.wait("tensor", r1)
                p.wait("tensor", self.pxbuf.free)
                tr1 = p.emit("tensor", lambda e: e.transpose(
                    out=psS.ap[:], in_=m1col.ap[:], identity=self.id_f32[:]), inc="s_pe")
                m1col.free = tr1
                p.wait("vector", tr1)
                p.wait("vector", murow.free)
                c1m = p.emit("vector", lambda e: e.tensor_copy(out=murow.ap[:], in_=psS.ap[:]),
                             inc="s_dve")
                self.pxbuf.free = c1m
                p.wait("vector", c1m)
                r2 = p.emit("vector", lambda e: e.reduce_max(
                    out=nsmall.ap[:], in_=murow.ap[:], axis=AX.X, negate=True), inc="s_dve")
                murow.free = r2
                p.wait("tensor", r2)
                p.wait("tensor", self.pxbuf.free)
                nbm = p.emit("tensor", lambda e: e.matmul(
                    out=psNB.ap[:], lhsT=self.onesf_row[0:1, :],
                    rhs=nsmall.ap[0:1, 0:1], start=True, stop=True), inc="s_pe")
                nsmall.free = nbm
                p.wait("vector", nbm)
                p.wait("vector", nb.free)
                cnb = p.emit("vector", lambda e: e.tensor_copy(out=nb.ap[:], in_=psNB.ap[:]),
                             inc="s_dve")
                self.pxbuf.free = cnb
                nb.ready = cnb

                p.wait("scalar", nb.ready)
                last_r = None
                last_av = None
                for st in range(NST):
                    qs = max(0, (st - 8 * pas) * 128)
                    pp = psA if st % 2 == 0 else psB
                    p.wait("tensor", pp.free)
                    cols = [(c0, min(c0 + 512, P)) for c0 in range((qs // 512) * 512, P, 512)]
                    mms = None
                    for (c0, c1) in cols:
                        mms = p.emit("tensor", lambda e, st=st, c0=c0, c1=c1, pp=pp, qb=qb, kT=kT:
                                     e.matmul(out=pp.ap[:, c0:c1],
                                              lhsT=kT.ap[:, st * 128:(st + 1) * 128],
                                              rhs=qb.ap[:, c0:c1], start=True, stop=True),
                                     inc="s_pe" if (c0, c1) == cols[-1] else None)
                    sc, pb = scf[st % 2], pt[st % 2]
                    p.wait("vector", mms)
                    p.wait("vector", sc.free)
                    cpy = p.emit("vector", lambda e, sc=sc, pp=pp, qs=qs: e.tensor_copy(
                        out=sc.ap[:, qs:P], in_=pp.ap[:, qs:P]), inc="s_dve")
                    pp.free = cpy
                    p.wait("scalar", cpy)
                    p.wait("scalar", pb.free)
                    eexp = p.emit("scalar", lambda e, sc=sc, pb=pb, qs=qs, nb=nb: e.activation(
                        out=pb.ap[:, qs:P], in_=sc.ap[:, qs:P], func=AF.Exp, bias=nb.ap[:]),
                        inc="s_act")
                    sc.free = eexp
                    pmarks = [eexp]
                    if qs > 0:
                        p.wait("vector", eexp)
                        z1 = p.emit("vector", lambda e, pb=pb, qs=qs: e.memset(
                            pb.ap[:, 0:qs], 0.0), inc="s_dve")
                        pmarks.append(z1)
                    if 0 <= (st - 8 * pas) <= NQT - 1 and (st - 8 * pas) * 128 == qs:
                        p.wait("vector", eexp)
                        dmk = p.emit("vector", lambda e, pb=pb, qs=qs: e.tensor_tensor(
                            out=pb.ap[:, qs:qs + 128], in0=pb.ap[:, qs:qs + 128],
                            in1=self.diag[:], op=ALU.mult), inc="s_dve")
                        pmarks.append(dmk)
                    for m in pmarks:
                        p.wait("tensor", m)
                    vt = vown[st - 8 * pas] if st >= 8 * pas else vstr[st]
                    p.wait("tensor", vt.ready)
                    if st == 0:
                        p.wait("tensor", psD.free)
                        p.wait("tensor", self.pxbuf.free)
                    for c in range(2):
                        last_av = p.emit("tensor", lambda e, c=c, vt=vt, pb=pb, st=st: e.matmul(
                            out=psD.ap[:, c * 512:(c + 1) * 512], lhsT=vt.ap[:],
                            rhs=pb.ap[:, c * 512:(c + 1) * 512],
                            start=(st == 0), stop=(st == NST - 1)),
                            inc="s_pe" if (st == NST - 1 and c == 1) else None)
                    for c in range(2):
                        pr = psR.ap[0:1, :] if c == 0 else psR.ap[32:33, :]
                        last_r = p.emit("tensor", lambda e, c=c, pb=pb, st=st, pr=pr: e.matmul(
                            out=pr, lhsT=self.ones_col[:],
                            rhs=pb.ap[:, c * 512:(c + 1) * 512],
                            start=(st == 0), stop=(st == NST - 1)),
                            inc="s_pe" if (st == NST - 1 and c == 1) else None)
                    pb.free = p.mark("s_pe")
                    if st >= 8 * pas:
                        vt.ready = None
                av_done = p.mark("s_pe")
                kT.free = [av_done, kstm] if kstm else av_done
                qb.free = av_done
                nb.free = p.mark("s_act")
                for st8 in range(NQT):
                    vown[st8].free = ([vown[st8].free, av_done]
                                      if (pas == 0 and vown[st8].free) else av_done)
                    vstr[st8].free = av_done

                p.wait("vector", last_r)
                p.wait("vector", rrow.free)
                rc1 = p.emit("vector", lambda e: e.tensor_copy(
                    out=rrow.ap[0:1, 0:512], in_=psR.ap[0:1, :]), inc="s_dve")
                rc2 = p.emit("vector", lambda e: e.tensor_copy(
                    out=rrow.ap[0:1, 512:1024], in_=psR.ap[32:33, :]), inc="s_dve")
                self.pxbuf.free = rc2
                p.wait("scalar", rc1)
                p.wait("scalar", rc2)
                p.wait("scalar", rec.free)
                l1 = p.emit("scalar", lambda e: e.activation(
                    out=rec.ap[:], in_=rrow.ap[:], func=AF.Ln), inc="s_act")
                rrow.free = l1
                p.wait("scalar", l1)
                l2 = p.emit("scalar", lambda e: e.activation(
                    out=rec.ap[:], in_=rec.ap[:], func=AF.Exp,
                    scale=self.c_neg1[0:1, 0:1]), inc="s_act")
                p.wait("tensor", l2)
                p.wait("tensor", psA.free)
                rb = None
                for c in range(2):
                    rb = p.emit("tensor", lambda e, c=c: e.matmul(
                        out=psA.ap[:, c * 512:(c + 1) * 512], lhsT=self.onesf_row[0:1, :],
                        rhs=rec.ap[0:1, c * 512:(c + 1) * 512], start=True, stop=True),
                        inc="s_pe" if c == 1 else None)
                rec.free = rb
                p.wait("vector", rb)
                p.wait("vector", recb.free)
                rcb = p.emit("vector", lambda e: e.tensor_copy(out=recb.ap[:], in_=psA.ap[:]),
                             inc="s_dve")
                psA.free = rcb
                p.wait("vector", rcb)
                p.wait("vector", av_done)
                p.wait("vector", yev.free)
                ym = p.emit("vector", lambda e: e.tensor_tensor(
                    out=yev.ap[:], in0=psD.ap[:], in1=recb.ap[:], op=ALU.mult), inc="s_dve")
                psD.free = ym
                recb.free = ym
                p.wait("gpsimd", ym)
                ydma = p.emit("gpsimd", lambda e, h=h: e.dma_start(
                    out=yst[h], in_=yev.ap[:]), inc="d_yst", amt=16)
                yev.free = ydma
                ymarks[h] = ydma

            heads_end = {"pe": p.mark("s_pe"), "dve": p.mark("s_dve"), "act": p.mark("s_act")}
            self.es = outer_es

        with ExitStack() as op_scope:
            outer_es, self.es = self.es, op_scope
            wof = B(sb("wof", (128, NK * D), BF16))
            bv_sb = sb("bv_sb", (128, NK), BF16)
            bo_sb = sb("bo_sb", (1, D), BF16)
            browf = sb("browf", (1, 1024), F32)
            brow = sb("brow", (1, D), BF16)
            yqt = [B(sb(f"yqt{i}", (128, H * 128), BF16)) for i in range(2)]

            for q in ("gpsimd", "sync"):
                p.wait(q, heads_end["pe"])
                p.wait(q, heads_end["dve"])
                p.wait(q, heads_end["act"])
                p.wait_dma_all(q)
            for k in range(NK):
                p.emit("gpsimd", lambda e, k=k: e.dma_start(
                    out=wof.ap[:, k * D:(k + 1) * D], in_=self.ap[layer + "_wo"][k]),
                    inc="d_wof", amt=16)
            for n, dst in ((layer + "_bv", bv_sb), (layer + "_bo", bo_sb)):
                p.emit("gpsimd", lambda e, n=n, d=dst: e.dma_start(out=d[:], in_=self.ap[n][:]),
                       inc="d_bias", amt=16)
            wof_m = [p.mark("d_wof"), p.mark("d_bias")]

            p.wait("tensor", wof_m)
            p.wait("tensor", self.pxbuf.free)
            mm = None
            for half in range(2):
                for k in range(NK):
                    for c in range(2):
                        pr = psR.ap[0:1, :] if c == 0 else psR.ap[32:33, :]
                        mm = p.emit("tensor", lambda e, k=k, c=c, half=half, pr=pr: e.matmul(
                            out=pr, lhsT=bv_sb[:, k:k + 1],
                            rhs=wof.ap[:, k * D + half * 1024 + c * 512:
                                       k * D + half * 1024 + (c + 1) * 512],
                            start=(k == 0), stop=(k == NK - 1)),
                            inc="s_pe" if (k == NK - 1 and c == 1) else None)
                p.wait("vector", mm)
                cv1 = p.emit("vector", lambda e: e.tensor_copy(
                    out=browf[0:1, 0:512], in_=psR.ap[0:1, :]), inc="s_dve")
                cv2 = p.emit("vector", lambda e: e.tensor_copy(
                    out=browf[0:1, 512:1024], in_=psR.ap[32:33, :]), inc="s_dve")
                self.pxbuf.free = cv2
                p.wait("vector", cv2)
                p.wait("vector", wof_m[1])
                bvv = p.emit("vector", lambda e, half=half: e.tensor_tensor(
                    out=browf[0:1, :], in0=browf[0:1, :],
                    in1=bo_sb[0:1, half * 1024:(half + 1) * 1024], op=ALU.add),
                    inc="s_dve")
                p.wait("vector", bvv)
                brow_ready = p.emit("vector", lambda e, half=half: e.tensor_copy(
                    out=brow[0:1, half * 1024:(half + 1) * 1024], in_=browf[0:1, :]),
                    inc="s_dve")
                p.wait("tensor", brow_ready)

            for qt in range(NQT):
                yq = yqt[qt % 2]
                p.wait("sync", yq.free)
                if qt == 0:
                    p.wait("sync", p.mark("d_yst"))
                yq.ready = p.emit("sync", lambda e, qt=qt, yq=yq: e.dma_start(
                    out=yq.ap[:].rearrange("p (h c) -> p h c", h=H),
                    in_=yst[:, :, qt * 128:(qt + 1) * 128].rearrange("h p c -> p h c")),
                    inc=f"d_yqt{qt % 2}", amt=16)
                p.wait("tensor", yq.ready)
                p.wait("tensor", psA.free)
                p.wait("tensor", psB.free)
                mmq = None
                for k in range(NK):
                    for c in range(4):
                        pd = psA if c < 2 else psB
                        p.emit("tensor", lambda e, k=k, c=c, yq=yq, pd=pd: e.matmul(
                            out=pd.ap[:, (c % 2) * 512:((c % 2) + 1) * 512],
                            lhsT=yq.ap[:, k * 128:(k + 1) * 128],
                            rhs=wof.ap[:, k * D + c * 512: k * D + (c + 1) * 512],
                            start=(k == 0), stop=False))
                for c in range(4):
                    pd = psA if c < 2 else psB
                    mmq = p.emit("tensor", lambda e, c=c, pd=pd: e.matmul(
                        out=pd.ap[:, (c % 2) * 512:((c % 2) + 1) * 512],
                        lhsT=self.ones_row[0:1, 0:128],
                        rhs=brow[0:1, c * 512:(c + 1) * 512], start=False, stop=True),
                        inc="s_pe" if c == 3 else None)
                yq.free = mmq
                p.wait("vector", mmq)
                a1 = p.emit("vector", lambda e, qt=qt: e.tensor_tensor(
                    out=self.x_res[:, qt * D: qt * D + 1024],
                    in0=self.x_res[:, qt * D: qt * D + 1024],
                    in1=psA.ap[:], op=ALU.add), inc="s_dve")
                a2 = p.emit("vector", lambda e, qt=qt: e.tensor_tensor(
                    out=self.x_res[:, qt * D + 1024: qt * D + 2048],
                    in0=self.x_res[:, qt * D + 1024: qt * D + 2048],
                    in1=psB.ap[:], op=ALU.add), inc="s_dve")
                psA.free = a1
                psB.free = a2
            self.es = outer_es

    def emit_ffn(self, pas):
        p = self.p
        sb = self.sb
        B = Buf
        xlnT = self.xlnT
        psA, psB, psD, psT = self.psA, self.psB, self.psD, self.psT
        wg = [B(sb(f"wgs{i}", (128, NK * 128), BF16)) for i in range(2)]
        wu = [B(sb(f"wus{i}", (128, NK * 128), BF16)) for i in range(2)]
        wd = [B(sb(f"wds{i}", (128, NM * 1024), BF16)) for i in range(2)]
        hpr = B(sb("hpr", (128, NM * P), BF16))
        lg = B(sb("lg", (128, NE), F32))
        wkk = B(sb("wkk", (128, NE), F32))
        msk = B(sb("msk", (128, NE), F32))
        uu = B(sb("uu", (128, NE), F32))
        nm0 = B(sb("nm0", (128, 1), F32))
        nmi = B(sb("nmi", (128, 1), F32))
        den = B(sb("den", (128, 1), F32))
        rden = B(sb("rden", (128, 1), F32))
        wexp = sb("wexp", (128, NQT * NE), F32)
        g0 = Buf(self.lnscr[0].ap[:, 0:P])
        s0 = Buf(self.lnscr[0].ap[:, P:2 * P])
        dt0 = [Buf(self.lnscr[1].ap[:, 0:P]), Buf(self.lnscr[1].ap[:, P:2 * P])]
        g0.free = self.lnscr[0].free
        s0.free = self.lnscr[0].free
        dt0[0].free = self.lnscr[1].free
        dt0[1].free = self.lnscr[1].free

        p.wait("tensor", self.xlnT_ready)
        for qt in range(NQT):
            p.wait("tensor", self.psT.free)
            p.wait("tensor", self.pxbuf.free)
            mm = None
            for k in range(NK):
                mm = p.emit("tensor", lambda e, k=k, qt=qt: e.matmul(
                    out=psT.ap[:], lhsT=xlnT[:, k * P + qt * 128: k * P + qt * 128 + 128],
                    rhs=self.rw_sb[:, k * NE:(k + 1) * NE],
                    start=(k == 0), stop=(k == NK - 1)), inc="s_pe" if k == NK - 1 else None)
            p.wait("vector", mm)
            p.wait("vector", lg.free)
            c1 = p.emit("vector", lambda e, lg=lg: e.tensor_copy(out=lg.ap[:], in_=psT.ap[:]),
                        inc="s_dve")
            self.psT.free = c1
            self.pxbuf.free = c1
            p.wait("vector", c1)
            import os as _os2
            if _os2.environ.get("FFN_DEBUG") == "router":
                p.wait("gpsimd", c1)
                p.emit("gpsimd", lambda e, qt=qt: e.dma_start(
                    out=self.out[1152:1280, qt * NE:(qt + 1) * NE], in_=lg.ap[:]),
                    inc="d_kst", amt=16)
                p.wait("vector", p.mark("d_kst"))
            c2 = p.emit("vector", lambda e, wkk=wkk, lg=lg: e.tensor_copy(
                out=wkk.ap[:], in_=lg.ap[:]), inc="s_dve")
            p.wait("vector", c2)
            for i in range(8):
                tgt = nm0 if i == 0 else nmi
                mr = p.emit("vector", lambda e, tgt=tgt, wkk=wkk: e.reduce_max(
                    out=tgt.ap[:], in_=wkk.ap[:], axis=AX.X, negate=True), inc="s_dve")
                p.wait("vector", mr)
                if i < 7:
                    mk = p.emit("vector", lambda e, tgt=tgt, wkk=wkk, msk=msk: e.tensor_scalar(
                        out=msk.ap[:], in0=wkk.ap[:], scalar1=tgt.ap[:], scalar2=0.0,
                        op0=ALU.add, op1=ALU.is_ge), inc="s_dve")
                    p.wait("vector", mk)
                    mw = p.emit("vector", lambda e, wkk=wkk, msk=msk: e.scalar_tensor_tensor(
                        out=wkk.ap[:], in0=msk.ap[:], scalar=-1.0e30, in1=wkk.ap[:],
                        op0=ALU.mult, op1=ALU.add), inc="s_dve")
                    p.wait("vector", mw)
            kp = p.emit("vector", lambda e, lg=lg, msk=msk, nmi=nmi: e.tensor_scalar(
                out=msk.ap[:], in0=lg.ap[:], scalar1=nmi.ap[:], scalar2=0.0,
                op0=ALU.add, op1=ALU.is_ge), inc="s_dve")
            p.wait("scalar", p.mark("s_dve"))
            ue = p.emit("scalar", lambda e, uu=uu, lg=lg, nm0=nm0: e.activation(
                out=uu.ap[:], in_=lg.ap[:], func=AF.Exp, bias=nm0.ap[:]), inc="s_act")
            lg.free = ue
            p.wait("vector", ue)
            p.wait("vector", kp)
            up = p.emit("vector", lambda e, uu=uu, msk=msk: e.tensor_tensor(
                out=uu.ap[:], in0=uu.ap[:], in1=msk.ap[:], op=ALU.mult), inc="s_dve")
            p.wait("vector", up)
            dn = p.emit("vector", lambda e, den=den, uu=uu: e.reduce_sum(
                out=den.ap[:], in_=uu.ap[:], axis=AX.X), inc="s_dve")
            p.wait("vector", dn)
            rd = p.emit("vector", lambda e, rden=rden, den=den: e.reciprocal(
                out=rden.ap[:], in_=den.ap[:]), inc="s_dve")
            p.wait("vector", rd)
            wq = p.emit("vector", lambda e, qt=qt, uu=uu, rden=rden: e.tensor_scalar(
                out=wexp[:, qt * NE:(qt + 1) * NE], in0=uu.ap[:], scalar1=rden.ap[:],
                scalar2=None, op0=ALU.mult), inc="s_dve")
        wexp_ready = p.mark("s_dve")
        p.wait("scalar", wexp_ready)
        import os as _os
        if _os.environ.get("FFN_DEBUG") == "router":
            p.wait("gpsimd", wexp_ready)
            p.emit("gpsimd", lambda e: e.dma_start(
                out=self.out[1024:1152, 0:NQT * NE], in_=wexp[:]), inc="d_kst", amt=16)
            return

        for en in range(NE):
            for m in range(NM):
                sgm = wg[(en * NM + m) % 2]
                sum_ = wu[(en * NM + m) % 2]
                p.wait("gpsimd", sgm.free)
                sgm.ready = p.emit("gpsimd", lambda e, en=en, m=m, s=sgm: e.dma_start(
                    out=s.ap[:].rearrange("p (k g) -> p k g", k=NK),
                    in_=self.ap["f_wg"][en, m]), inc=f"d_wg{(en * NM + m) % 2}", amt=16)
                p.wait("gpsimd", sum_.free)
                sum_.ready = p.emit("gpsimd", lambda e, en=en, m=m, s=sum_: e.dma_start(
                    out=s.ap[:].rearrange("p (k g) -> p k g", k=NK),
                    in_=self.ap["f_wu"][en, m]), inc=f"d_wu{(en * NM + m) % 2}", amt=16)
                p.wait("tensor", sgm.ready)
                p.wait("tensor", psA.free)
                mmg = None
                for k in range(NK):
                    for c in range(2):
                        mmg = p.emit("tensor", lambda e, k=k, c=c, s=sgm: e.matmul(
                            out=psA.ap[:, c * 512:(c + 1) * 512],
                            lhsT=s.ap[:, k * 128:(k + 1) * 128],
                            rhs=xlnT[:, k * P + c * 512: k * P + (c + 1) * 512],
                            start=(k == 0), stop=(k == NK - 1)),
                            inc="s_pe" if (k == NK - 1 and c == 1) else None)
                sgm.free = mmg
                p.wait("tensor", sum_.ready)
                p.wait("tensor", psB.free)
                mmu = None
                for k in range(NK):
                    for c in range(2):
                        mmu = p.emit("tensor", lambda e, k=k, c=c, s=sum_: e.matmul(
                            out=psB.ap[:, c * 512:(c + 1) * 512],
                            lhsT=s.ap[:, k * 128:(k + 1) * 128],
                            rhs=xlnT[:, k * P + c * 512: k * P + (c + 1) * 512],
                            start=(k == 0), stop=(k == NK - 1)),
                            inc="s_pe" if (k == NK - 1 and c == 1) else None)
                sum_.free = mmu
                p.wait("scalar", mmg)
                p.wait("scalar", g0.free)
                cg = p.emit("scalar", lambda e, g0=g0: e.activation(
                    out=g0.ap[:], in_=psA.ap[:], func=AF.Identity), inc="s_act")
                psA.free = cg
                p.wait("scalar", cg)
                p.wait("scalar", s0.free)
                sl_ = p.emit("scalar", lambda e, g0=g0, s0=s0: e.activation(
                    out=s0.ap[:], in_=g0.ap[:], func=AF.Silu), inc="s_act")
                g0.free = sl_
                p.wait("vector", mmu)
                p.wait("vector", sl_)
                p.wait("vector", hpr.free)
                hm = p.emit("vector", lambda e, m=m, s0=s0, hpr=hpr: e.tensor_tensor(
                    out=hpr.ap[:, m * P:(m + 1) * P], in0=psB.ap[:], in1=s0.ap[:],
                    op=ALU.mult), inc="s_dve")
                psB.free = hm
                s0.free = hm
                if m == 0:
                    hpr_first = hm
            hpr_ready = p.mark("s_dve")
            for half in range(2):
                sd = wd[(en * 2 + half) % 2]
                p.wait("gpsimd", sd.free)
                sd.ready = p.emit("gpsimd", lambda e, en=en, half=half, s=sd: e.dma_start(
                    out=s.ap[:].rearrange("p (k c) -> p k c", k=NM),
                    in_=self.ap["f_wd"][en, half]), inc=f"d_wd{(en * 2 + half) % 2}", amt=16)
                p.wait("tensor", sd.ready)
                p.wait("tensor", hpr_ready)
                last_dn = None
                for qt in range(NQT):
                    p.wait("tensor", psD.free)
                    mmd = None
                    for k in range(NM):
                        for c in range(2):
                            mmd = p.emit("tensor", lambda e, k=k, c=c, qt=qt, s=sd, hpr=hpr:
                                         e.matmul(out=psD.ap[:, c * 512:(c + 1) * 512],
                                                  lhsT=hpr.ap[:, k * P + qt * 128:
                                                              k * P + qt * 128 + 128],
                                                  rhs=s.ap[:, k * 1024 + c * 512:
                                                           k * 1024 + (c + 1) * 512],
                                                  start=(k == 0), stop=(k == NM - 1)),
                                         inc="s_pe" if (k == NM - 1 and c == 1) else None)
                    dtmp = dt0[qt % 2]
                    p.wait("scalar", mmd)
                    p.wait("scalar", dtmp.free)
                    sc_ = p.emit("scalar", lambda e, qt=qt, en=en, dtmp=dtmp: e.activation(
                        out=dtmp.ap[:], in_=psD.ap[:], func=AF.Identity,
                        scale=wexp[:, qt * NE + en: qt * NE + en + 1]), inc="s_act")
                    psD.free = sc_
                    p.wait("vector", sc_)
                    ad = p.emit("vector", lambda e, qt=qt, half=half, dtmp=dtmp: e.tensor_tensor(
                        out=self.x_res[:, qt * D + half * 1024: qt * D + half * 1024 + 1024],
                        in0=self.x_res[:, qt * D + half * 1024: qt * D + half * 1024 + 1024],
                        in1=dtmp.ap[:], op=ALU.add), inc="s_dve")
                    dtmp.free = ad
                    last_dn = ad
                sd.free = p.mark("s_pe")
            hpr.free = p.mark("s_pe")
        self.lnscr[0].free = p.mark("s_dve")
        self.lnscr[1].free = p.mark("s_dve")


from concourse.bass_utils import run_bass_kernel_spmd


def kernel(**inputs):
    prep = host_prepare(inputs)
    builder = K()
    nc = builder.build()
    base = {k: v for k, v in prep.items() if not k.startswith("_")}
    in_maps = []
    for c in range(8):
        m = dict(base)
        m["xin"] = prep["_x_seqs"][c % 4]
        in_maps.append(m)
    res = run_bass_kernel_spmd(nc, in_maps, core_ids=list(range(8)))
    return np.stack([np.asarray(res.results[b]["o"]) for b in range(4)], axis=0)
